# version 16
# baseline (speedup 1.0000x reference)
import math
import os
import contextlib
import numpy as np
import ml_dtypes
import concourse.bass as bass
import concourse.mybir as mybir
from concourse.bass_utils import run_bass_kernel_spmd

F32 = mybir.dt.float32
BF16 = mybir.dt.bfloat16
I32 = mybir.dt.int32
AF = mybir.ActivationFunctionType
ALU = mybir.AluOpType
AX = mybir.AxisListType

S = 2048
D = 4096
NH = 32
HD = 128
NT = S // 128
NI = D // 128
SCALE = HD ** -0.5
EPS = 1e-5
NEG = -1e30
QW = 512

ENG = ("sync", "gpsimd", "tensor", "vector", "scalar")


def _alibi_slopes(n):
    start = 2.0 ** (-(2.0 ** (-(math.log2(n) - 3))))
    return np.array([start ** (i + 1) for i in range(n)], dtype=np.float32)


class Plan:
    def __init__(self):
        self.ops = {e: [] for e in ENG}
        self.cnt = {e: 0 for e in ENG}
        self.cnt["qs"] = 0
        self.cnt["qg"] = 0

    def emit(self, e, fn, waits=(), inc=None):
        for s_, v in waits:
            assert v <= self.cnt[s_], f"forward wait {s_} {v} > {self.cnt[s_]}"
        step = 0
        if inc is not None:
            step = 16 if inc in ("qs", "qg") else 1
            self.cnt[inc] += step
        self.ops[e].append((fn, tuple(waits), inc, step))

    def selfwait(self, e):
        self.emit(e, None, waits=[(e, self.cnt[e])])

    def barrier(self):
        for e in ENG:
            self.emit(e, self.bfns[e], inc=e)
        tgt = dict(self.cnt)
        for e in ENG:
            waits = [(o, tgt[o]) for o in ENG if o != e]
            waits += [("qs", tgt["qs"]), ("qg", tgt["qg"])]
            self.emit(e, None, waits=waits)

    def replay(self, name, eng, sems):
        for fn, waits, inc, step in self.ops[name]:
            for s_, v in waits:
                if v > 0:
                    eng.wait_ge(sems[s_], v)
            if fn is None:
                continue
            instr = fn(eng)
            if inc is not None and instr is not None:
                instr.then_inc(sems[inc], step)


def build_nc():
    nc = bass.Bass("TRN2", target_bir_lowering=False,
                   detect_race_conditions=False)

    x_h = nc.dram_tensor("x", [S, D], F32, kind="ExternalInput").ap()
    wq_h = nc.dram_tensor("Wq", [D, D], F32, kind="ExternalInput").ap()
    wkv_h = nc.dram_tensor("Wkv", [D, 2 * HD], F32, kind="ExternalInput").ap()
    wo_h = nc.dram_tensor("Wo", [D, D], F32, kind="ExternalInput").ap()
    w1_h = nc.dram_tensor("w1", [D, D], F32, kind="ExternalInput").ap()
    w3_h = nc.dram_tensor("w3", [D, D], F32, kind="ExternalInput").ap()
    gam_h = nc.dram_tensor("gamma_r", [128, D], F32, kind="ExternalInput").ap()
    bet_h = nc.dram_tensor("beta_r", [128, D], F32, kind="ExternalInput").ap()
    b1_h = nc.dram_tensor("b1_r", [128, D], F32, kind="ExternalInput").ap()
    b3_h = nc.dram_tensor("b3_r", [128, D], F32, kind="ExternalInput").ap()
    id_h = nc.dram_tensor("ident", [128, 128], BF16, kind="ExternalInput").ap()
    on1f_h = nc.dram_tensor("ones1f", [1, 128], F32, kind="ExternalInput").ap()
    on1b_h = nc.dram_tensor("ones1b", [1, 128], BF16, kind="ExternalInput").ap()
    onecb_h = nc.dram_tensor("onecb", [128, 1], BF16, kind="ExternalInput").ap()
    on128_h = nc.dram_tensor("ones128", [128, 128], BF16, kind="ExternalInput").ap()
    tri_h = nc.dram_tensor("trimask", [128, 128], F32, kind="ExternalInput").ap()
    ramp_h = nc.dram_tensor("rampT", [128, NH * NT], F32, kind="ExternalInput").ap()
    fc_h = nc.dram_tensor("fconst", [128, 8], F32, kind="ExternalInput").ap()
    slq_h = nc.dram_tensor("slopeq", [NH, S], F32, kind="ExternalInput").ap()

    out_h = nc.dram_tensor("out", [S, D], F32, kind="ExternalOutput").ap()

    _dbg = "ExternalOutput" if os.environ.get("KDBG") else "Internal"
    st1_s = nc.dram_tensor("st1_s", [NT, 128, 16], F32, kind=_dbg).ap()
    xnT_s = nc.dram_tensor("xnT_s", [NI, 128, S], BF16, kind=_dbg).ap()
    hqT_s = nc.dram_tensor("hqT_s", [NI, 128, S], BF16, kind=_dbg).ap()
    h2_s = nc.dram_tensor("h2_s", [S, D], BF16, kind=_dbg).ap()
    h2qT_s = nc.dram_tensor("h2qT_s", [NI, 128, S], BF16, kind=_dbg).ap()
    g_s = nc.dram_tensor("g_s", [S, D], F32, kind=_dbg).ap()
    aot_s = nc.dram_tensor("aot_s", [NH, 128, S], BF16, kind=_dbg).ap()
    w1T_s = nc.dram_tensor("w1T_s", [8, 128, NI * 512], BF16, kind="Internal").ap()
    w3T_s = nc.dram_tensor("w3T_s", [8, 128, NI * 512], BF16, kind="Internal").ap()

    es = contextlib.ExitStack()
    with es:
        sb = lambda name, shp, dt: es.enter_context(nc.sbuf_tensor(name, shp, dt))
        kt_r = sb("kt_r", [128, S], BF16)
        v_r = sb("v_r", [128, NT * 128], BF16)
        ident = sb("identt", [128, 128], BF16)
        on1f = sb("on1f_s", [1, 128], F32)
        on1b = sb("on1b_s", [1, 128], BF16)
        onecb = sb("onecb_s", [128, 1], BF16)
        on128 = sb("on128_s", [128, 128], BF16)
        tri = sb("tri_s", [128, 128], F32)
        ramp = sb("ramp_s", [128, NH * NT], F32)
        mxk = sb("mxk", [1, 8], F32)
        fc = sb("fc", [128, 8], F32)
        rows = sb("rows", [1, 1536], F32)
        crow, brow = rows[:, 0:512], rows[:, 512:1024]
        slqc = rows[:, 1024:1536]
        rrow = rows[:, 0:512]
        evrow = rows[:, 512:1024]
        big = sb("big", [128, NI * 1024], BF16)
        stg = sb("stg", [128, NI * 512], BF16)
        pw = sb("pw", [128, 8192], BF16)
        xb = [sb(f"xb{i}", [128, D], F32) for i in range(2)]
        ln1 = sb("ln1", [128, D], F32)
        xnb = sb("xnb", [128, D], BF16)
        hqb = sb("hqb", [128, D], BF16)
        tq = sb("tq", [128, QW], F32)
        qi = sb("qi", [128, QW], I32)
        qf = sb("qf", [128, QW], F32)
        dq = sb("dq", [128, QW], F32)
        sg1 = sb("sg1", [128, QW], F32)
        sg2 = sb("sg2", [128, QW], F32)
        st1 = sb("st1", [128, 16], F32)
        ep = xb[0][:, 0:512]
        gb = xb[1][:, 0:512]
        gam = big[:, 16384:20480]
        bet = big[:, 20480:24576]
        wkv = pw[:, :]
        wb = [pw[:, 0:D], pw[:, D:2 * D]]
        pt_sb = pw[:, :]
        qt_sb = hqb[:, 0:512]
        qsq = hqb[:, 512:1024]
        ksq = hqb[:, 1024:1536]
        aotb = xnb[:, 0:512]
        h2b = xnb[:, 0:512]
        brd = ln1[:, 0:512]
        xsl = [tq, qf]
        gsl = [dq, sg1]
        obf = [sg2, gb]
        wqh = stg[:, 0:NI * 128]
        crow_b = stg[0:1, NI * 128:NI * 128 + 512]
        rrow_b = stg[0:1, NI * 128 + 512:NI * 128 + 1024]

        ps = lambda name, shp, dt: es.enter_context(nc.psum_tensor(name, shp, dt))
        pA = ps("pA", [128, 512], F32)
        pB = ps("pB", [128, 512], F32)
        pC = ps("pC", [128, 512], F32)
        pD = ps("pD", [128, 512], F32)
        pbank = [pA, pB, pC, pD]
        ptpf = [ps(f"ptp{i}", [128, 1024], BF16) for i in range(2)]
        ptp = [t[:, 0:128] for t in ptpf]
        prow = ps("prow", [1, 512], F32)

        sems = {}
        for s_ in list(ENG) + ["qs", "qg"]:
            sems[s_] = es.enter_context(nc.semaphore(s_))

        FCVALS = [-0.5, 0.5, EPS, -1.0, 1.0, 0.0, SCALE, 127.0]
        for fi, fv in enumerate(FCVALS):
            nc.const_aps.register(F32, fv, fc[:, fi:fi + 1])

        P = Plan()
        P.bfns = {
            "sync": lambda eng: eng.nop(),
            "gpsimd": lambda eng: eng.engine_nop(),
            "tensor": lambda eng: eng.ldweights(ident[:, 0:1]),
            "vector": lambda eng: eng.tensor_copy(st1[:, 3:4], st1[:, 2:3]),
            "scalar": lambda eng: eng.copy(st1[:, 15:16], st1[:, 14:15]),
        }
        E = P.emit

        def ld(e, dst, src, waits=(), q="qs"):
            E(e, lambda eng: eng.dma_start(out=dst, in_=src), waits=waits, inc=q)

        ld("sync", fc[:], fc_h[:])
        ld("sync", ident[:], id_h[:])
        ld("sync", on1f[:], on1f_h[:])
        ld("sync", on1b[:], on1b_h[:])
        ld("sync", onecb[:], onecb_h[:])
        ld("sync", on128[:], on128_h[:])
        ld("sync", tri[:], tri_h[:])
        ld("sync", ramp[:], ramp_h[:])
        ld("gpsimd", gam, gam_h[:], q="qg")
        ld("gpsimd", bet, bet_h[:], q="qg")
        ld("gpsimd", wkv[:, 0:NI * 256].rearrange("p (i c) -> p i c", i=NI),
           wkv_h.rearrange("(i p) c -> p i c", p=128), q="qg")
        n_const = P.cnt["qs"]
        qg_const = P.cnt["qg"]

        def quant(src_ap, n_cols, outb_ap):
            am, sq_, rq_ = st1[:, 0:1], st1[:, 1:2], st1[:, 2:3]
            E("vector", lambda eng: eng.tensor_reduce(
                am, src_ap, axis=AX.X, op=ALU.max, apply_absolute_value=True),
                inc="vector")
            P.selfwait("vector")
            E("vector", lambda eng: eng.tensor_scalar(
                sq_, am, 1.0 / 127.0, 1e-8, ALU.mult, ALU.max),
                waits=[("scalar", P.cnt["scalar"])], inc="vector")
            P.selfwait("vector")
            E("vector", lambda eng: eng.reciprocal(rq_, sq_), inc="vector")
            P.selfwait("vector")
            sq0 = P.cnt["scalar"]
            for j in range(n_cols // QW):
                sl = slice(j * QW, (j + 1) * QW)
                src_j = src_ap[:, sl]
                E("vector", lambda eng, s__=src_j: eng.tensor_scalar(
                    tq[:], s__, rq_, None, ALU.mult), inc="vector")
                E("vector", lambda eng: eng.tensor_copy(qi[:], tq[:]),
                  inc="vector")
                E("vector", lambda eng: eng.tensor_copy(qf[:], qi[:]),
                  inc="vector")
                E("vector", lambda eng: eng.tensor_tensor(
                    dq[:], tq[:], qf[:], ALU.subtract), inc="vector")
                vq2 = P.cnt["vector"]
                E("scalar", lambda eng: eng.activation(
                    sg1[:], dq[:], AF.Sign, bias=-0.5),
                    waits=[("vector", vq2)], inc="scalar")
                E("scalar", lambda eng: eng.activation(
                    sg2[:], dq[:], AF.Sign, bias=0.5), inc="scalar")
                sq2 = P.cnt["scalar"]
                E("vector", lambda eng: eng.scalar_tensor_tensor(
                    dq[:], sg2[:], 0.5, qf[:], ALU.mult, ALU.add),
                    waits=[("scalar", sq2)], inc="vector")
                E("vector", lambda eng: eng.scalar_tensor_tensor(
                    qf[:], sg1[:], 0.5, dq[:], ALU.mult, ALU.add), inc="vector")
                E("vector", lambda eng, s__=sl: eng.tensor_scalar(
                    outb_ap[:, s__], qf[:], sq_, None, ALU.mult), inc="vector")

        def transpose_to(src_bf, dst_col_fn, first_waits):
            for i in range(NI):
                pt = ptp[i % 2]
                w = list(first_waits) if i < 2 else \
                    [("vector", P.cnt["vector"])]
                E("tensor", lambda eng, i_=i, pt_=pt: eng.transpose(
                    pt_, src_bf[:, i_ * 128:(i_ + 1) * 128], ident[:]),
                    waits=w, inc="tensor")
                tc = P.cnt["tensor"]
                E("vector", lambda eng, i_=i, pt_=pt: eng.tensor_copy(
                    dst_col_fn(i_), pt_), waits=[("tensor", tc)],
                    inc="vector")

        PHLIM = int(os.environ.get("KPHASES", "9"))
        for e_ in ("tensor", "vector", "scalar"):
            E(e_, None, waits=[("qs", n_const), ("qg", qg_const)])
        hstg = big[:, 0:NI * 512]
        for t in range(NT if PHLIM >= 1 else 0):
            par = t % 2
            xbuf = xb[par]
            w = [] if t < 2 else [("scalar", P.cnt["scalar"]),
                                  ("vector", P.cnt["vector"])]
            ld("sync", xbuf[:], x_h[t * 128:(t + 1) * 128, :], waits=w)
            nqs = P.cnt["qs"]
            E("vector", None, waits=[("scalar", P.cnt["scalar"]),
                                     ("tensor", P.cnt["tensor"])])
            E("vector", lambda eng, xb_=xbuf: eng.tensor_reduce(
                st1[:, 4:5], xb_[:], axis=AX.X, op=ALU.add),
                waits=[("qs", nqs)], inc="vector")
            P.selfwait("vector")
            E("vector", lambda eng: eng.tensor_scalar(
                st1[:, 5:6], st1[:, 4:5], -1.0 / D, None, ALU.mult),
                inc="vector")
            vc = P.cnt["vector"]
            E("scalar", lambda eng, xb_=xbuf: eng.activation(
                ln1[:], xb_[:], AF.Square, accum_out=st1[:, 6:7]),
                waits=[("vector", vc)], inc="scalar")
            sc0 = P.cnt["scalar"]
            P.selfwait("vector")
            E("vector", lambda eng: eng.tensor_tensor(
                st1[:, 12:13], st1[:, 5:6], st1[:, 5:6], ALU.mult),
                waits=[("scalar", sc0)], inc="vector")
            P.selfwait("vector")
            E("vector", lambda eng: eng.scalar_tensor_tensor(
                st1[:, 13:14], st1[:, 6:7], 1.0 / D, st1[:, 12:13], ALU.mult,
                ALU.subtract), inc="vector")
            P.selfwait("vector")
            E("vector", lambda eng: eng.tensor_scalar(
                st1[:, 15:16], st1[:, 13:14], 1.0, EPS, ALU.mult, ALU.add),
                inc="vector")
            P.selfwait("vector")
            E("vector", lambda eng: eng.reciprocal(
                st1[:, 7:8], st1[:, 15:16]), inc="vector")
            vv_ = P.cnt["vector"]
            E("scalar", lambda eng: eng.activation(
                st1[:, 8:9], st1[:, 7:8], AF.Sqrt),
                waits=[("vector", vv_)], inc="scalar")
            sc1 = P.cnt["scalar"]
            P.selfwait("vector")
            E("vector", lambda eng: eng.tensor_tensor(
                st1[:, 14:15], st1[:, 5:6], st1[:, 8:9], ALU.mult),
                waits=[("scalar", sc1)], inc="vector")
            E("vector", lambda eng, xb_=xbuf: eng.tensor_scalar(
                ln1[:], xb_[:], st1[:, 8:9], None, ALU.mult),
                inc="vector")
            P.selfwait("vector")
            E("vector", lambda eng: eng.tensor_scalar(
                xnb[:], ln1[:], st1[:, 14:15], None, ALU.add),
                waits=[("tensor", P.cnt["tensor"])], inc="vector")
            E("scalar", lambda eng: eng.activation(
                st1[:, 9:10], st1[:, 13:14], AF.Copy, scale=st1[:, 7:8],
                bias=EPS), inc="scalar")
            sc2a = P.cnt["scalar"]
            E("vector", lambda eng: eng.reciprocal(
                st1[:, 11:12], st1[:, 9:10]), waits=[("scalar", sc2a)],
                inc="vector")
            vg2 = P.cnt["vector"]
            E("scalar", lambda eng: eng.activation(
                st1[:, 10:11], st1[:, 11:12], AF.Sqrt),
                waits=[("vector", vg2)], inc="scalar")
            P.selfwait("scalar")
            sc2 = P.cnt["scalar"]
            E("vector", lambda eng: eng.tensor_tensor(
                ln1[:], xnb[:], gam, ALU.mult), waits=[("scalar", sc2)],
                inc="vector")
            vl = P.cnt["vector"]
            E("scalar", lambda eng: eng.activation(
                ln1[:], ln1[:], AF.Copy, scale=st1[:, 10:11]),
                waits=[("vector", vl)], inc="scalar")
            sl_ = P.cnt["scalar"]
            E("vector", lambda eng: eng.tensor_tensor(
                ln1[:], ln1[:], bet, ALU.add), waits=[("scalar", sl_)],
                inc="vector")
            if os.environ.get("KDBG"):
                E("gpsimd", lambda eng, t_=t: eng.dma_start(
                    out=st1_s[t_, :, :], in_=st1[:, :]),
                    waits=[("scalar", P.cnt["scalar"]),
                           ("vector", P.cnt["vector"])], inc="qg")
            quant(ln1[:], D, hqb[:])
            cpos = (t % 4) * 128
            transpose_to(
                xnb, lambda i, c_=cpos: stg[:, i * 512 + c_:i * 512 + c_ + 128],
                [("scalar", sc2)])
            vhq = P.cnt["vector"]
            transpose_to(
                hqb, lambda i, c_=cpos: hstg[:, i * 512 + c_:i * 512 + c_ + 128],
                [("vector", vhq)])

            if t % 4 == 3:
                c = t // 4
                vkv = P.cnt["vector"]
                for i in range(NI):
                    wv = [("vector", vkv), ("qg", qg_const)] if i == 0 else ()
                    E("tensor", lambda eng, i_=i: eng.matmul(
                        pA[:, :], wkv[:, i_ * 256:i_ * 256 + 128],
                        stg[:, i_ * 512:(i_ + 1) * 512],
                        start=(i_ == 0), stop=(i_ == NI - 1)), waits=wv,
                        inc="tensor")
                for i in range(NI):
                    E("tensor", lambda eng, i_=i: eng.matmul(
                        pB[:, :], wkv[:, i_ * 256 + 128:(i_ + 1) * 256],
                        stg[:, i_ * 512:(i_ + 1) * 512],
                        start=(i_ == 0), stop=(i_ == NI - 1)), inc="tensor")
                tkv = P.cnt["tensor"]
                E("vector", lambda eng, c_=c: eng.tensor_copy(
                    kt_r[:, c_ * 512:(c_ + 1) * 512], pA[:, :]),
                    waits=[("tensor", tkv)], inc="vector")
                E("vector", lambda eng: eng.tensor_copy(
                    xnb[:, 0:512], pB[:, :]), inc="vector")
                vvt = P.cnt["vector"]
                for j in range(4):
                    pt = ptp[j % 2]
                    wv = [("vector", vvt)] if j < 2 else \
                        [("vector", P.cnt["vector"])]
                    E("tensor", lambda eng, j_=j, pt_=pt: eng.transpose(
                        pt_, xnb[:, j_ * 128:(j_ + 1) * 128], ident[:]),
                        waits=wv, inc="tensor")
                    tv2 = P.cnt["tensor"]
                    E("vector", lambda eng, c_=c, j_=j, pt_=pt: eng.tensor_copy(
                        v_r[:, (c_ * 4 + j_) * 128:(c_ * 4 + j_ + 1) * 128],
                        pt_), waits=[("tensor", tv2)], inc="vector")
                vsp = P.cnt["vector"]
                for i in range(NI):
                    E("gpsimd", lambda eng, c_=c, i_=i: eng.dma_start(
                        out=xnT_s[i_, :, c_ * 512:(c_ + 1) * 512],
                        in_=stg[:, i_ * 512:(i_ + 1) * 512]),
                        waits=[("vector", vsp)] if i == 0 else (), inc="qg")
                    E("gpsimd", lambda eng, c_=c, i_=i: eng.dma_start(
                        out=hqT_s[i_, :, c_ * 512:(c_ + 1) * 512],
                        in_=hstg[:, i_ * 512:(i_ + 1) * 512]), inc="qg")
                if t < NT - 1:
                    qgc = P.cnt["qg"]
                    E("tensor", None, waits=[("qg", qgc)])

        P.barrier()

        def reload_big_half(src, half, width=1024):
            tw = [("tensor", P.cnt["tensor"])]
            for i in range(NI):
                ld("sync", big[:, i * width:(i + 1) * width],
                   src[i, :, half * width:(half + 1) * width],
                   waits=tw if i == 0 else ())
            return P.cnt["qs"]

        def ffn_pass(w_h, wT_s, b_sb, act_fn, dst_s, dst_f32, src_s):
            for half in range(2):
                qs_big = reload_big_half(src_s, half)
                for oc in range(8):
                    if half == 0:
                        for j in range(4):
                            wbuf = wb[j % 2]
                            r0 = oc * 512 + j * 128
                            wv = [("vector", P.cnt["vector"]),
                                  ("tensor", P.cnt["tensor"])]
                            ld("gpsimd", wbuf[:], w_h[r0:r0 + 128, :],
                               waits=wv, q="qg")
                            nqg = P.cnt["qg"]
                            E("vector", None, waits=[("qg", nqg),
                                                     ("tensor", P.cnt["tensor"])])
                            quant(wbuf[:], D, hqb[:])
                            transpose_to(
                                hqb, lambda i, j_=j: stg[
                                    :, i * 512 + j_ * 128:i * 512 + j_ * 128 + 128],
                                [("vector", P.cnt["vector"])])
                        vstg = P.cnt["vector"]
                        E("gpsimd", lambda eng, oc_=oc: eng.dma_start(
                            out=wT_s[oc_, :, :], in_=stg[:, :]),
                            waits=[("vector", vstg)], inc="qg")
                    else:
                        wv = [("tensor", P.cnt["tensor"])]
                        ld("sync", stg[:, :], wT_s[oc, :, :], waits=wv)
                        vstg = P.cnt["vector"]
                    qs_w = P.cnt["qs"]
                    qg_w = P.cnt["qg"]
                    for grp in range(2):
                        for i in range(NI):
                            for tt in range(4):
                                t_ = grp * 4 + tt
                                wv3 = []
                                if i == 0 and tt == 0:
                                    wv3 = [("vector", vstg), ("qs", qs_big),
                                           ("qs", qs_w), ("qg", qg_w),
                                           ("scalar", P.cnt["scalar"])]
                                E("tensor", lambda eng, i_=i, t__=t_, tt_=tt:
                                    eng.matmul(
                                        pbank[tt_][:, :],
                                        big[:, i_ * 1024 + t__ * 128:
                                            i_ * 1024 + t__ * 128 + 128],
                                        stg[:, i_ * 512:(i_ + 1) * 512],
                                        start=(i_ == 0), stop=(i_ == NI - 1)),
                                    waits=wv3, inc="tensor")
                        tmm = P.cnt["tensor"]
                        for tt in range(4):
                            t_ = half * 8 + grp * 4 + tt
                            E("vector", lambda eng, tt_=tt, oc_=oc:
                                eng.tensor_tensor(
                                    gb[:], pbank[tt_][:, :],
                                    b_sb[:, oc_ * 512:(oc_ + 1) * 512], ALU.add),
                                waits=[("tensor", tmm),
                                       ("scalar", P.cnt["scalar"])],
                                inc="vector")
                            vep = P.cnt["vector"]
                            buf = ep if dst_f32 else h2b
                            E("scalar", lambda eng, b__=buf: eng.activation(
                                b__[:], gb[:], act_fn),
                                waits=[("vector", vep)], inc="scalar")
                            scv = P.cnt["scalar"]
                            E("gpsimd", lambda eng, t__=t_, oc_=oc, b__=buf:
                                eng.dma_start(
                                    out=dst_s[t__ * 128:(t__ + 1) * 128,
                                              oc_ * 512:(oc_ + 1) * 512],
                                    in_=b__[:]),
                                waits=[("scalar", scv)], inc="qg")
                            qgb = P.cnt["qg"]
                            E("scalar", None, waits=[("qg", qgb)])

        if PHLIM >= 2:
            ld("sync", ln1[:], b1_h[:])
            qsb1 = P.cnt["qs"]
            E("vector", None, waits=[("qs", qsb1)])
            ffn_pass(w1_h, w1T_s, ln1, AF.Gelu, h2_s, False, hqT_s)
        P.barrier()

        for t in range(NT if PHLIM >= 3 else 0):
            wv = [] if t == 0 else [("vector", P.cnt["vector"])]
            ld("sync", xnb[:], h2_s[t * 128:(t + 1) * 128, :], waits=wv)
            nq2 = P.cnt["qs"]
            E("vector", None, waits=[("qs", nq2),
                                     ("tensor", P.cnt["tensor"])])
            quant(xnb[:], D, hqb[:])
            cpos = (t % 4) * 128
            transpose_to(
                hqb, lambda i, c_=cpos: stg[:, i * 512 + c_:i * 512 + c_ + 128],
                [("vector", P.cnt["vector"])])
            if t % 4 == 3:
                c = t // 4
                vsp2 = P.cnt["vector"]
                for i in range(NI):
                    E("gpsimd", lambda eng, c_=c, i_=i: eng.dma_start(
                        out=h2qT_s[i_, :, c_ * 512:(c_ + 1) * 512],
                        in_=stg[:, i_ * 512:(i_ + 1) * 512]),
                        waits=[("vector", vsp2)] if i == 0 else (), inc="qg")
                if t < NT - 1:
                    qgc2 = P.cnt["qg"]
                    E("tensor", None, waits=[("qg", qgc2)])

        P.barrier()
        if PHLIM >= 4:
            ld("sync", ln1[:], b3_h[:])
            qsb3 = P.cnt["qs"]
            E("vector", None, waits=[("qs", qsb3)])
            ffn_pass(w3_h, w3T_s, ln1, AF.Silu, g_s, True, h2qT_s)
        P.barrier()

        for c in range(4 if PHLIM >= 5 else 0):
            wv = [("vector", P.cnt["vector"])] if c == 0 else \
                [("tensor", P.cnt["tensor"])]
            E("scalar", lambda eng, c_=c: eng.activation(
                ksq[:], kt_r[:, c_ * 512:(c_ + 1) * 512], AF.Square),
                waits=wv, inc="scalar")
            scq = P.cnt["scalar"]
            E("tensor", lambda eng: eng.matmul(
                prow[:, :], onecb[:], ksq[:], start=True, stop=True),
                waits=[("scalar", scq)], inc="tensor")
            tkq = P.cnt["tensor"]
            E("vector", lambda eng: eng.tensor_copy(evrow, prow[:, :]),
              waits=[("tensor", tkq)], inc="vector")
            P.selfwait("vector")
            E("vector", lambda eng, c_=c: eng.tensor_reduce(
                mxk[0:1, c_:c_ + 1], evrow, axis=AX.X, op=ALU.max),
                inc="vector")
        P.selfwait("vector")
        E("vector", lambda eng: eng.tensor_reduce(
            mxk[0:1, 4:5], mxk[0:1, 0:4], axis=AX.X, op=ALU.max), inc="vector")
        vmx = P.cnt["vector"]

        for half in range(2 if PHLIM >= 5 else 0):
            qs_big3 = reload_big_half(xnT_s, half)
            for h in range(NH):
                wv = [("tensor", P.cnt["tensor"])]
                for b_ in range(4):
                    E("gpsimd", lambda eng, h_=h, b__=b_: eng.dma_start(
                        out=wqh[:, b__ * 8 * 128:(b__ + 1) * 8 * 128].rearrange(
                            "p (i c) -> p i c", i=8),
                        in_=wq_h[b__ * 8 * 128:(b__ + 1) * 8 * 128,
                                 h_ * 128:(h_ + 1) * 128].rearrange(
                            "(i p) c -> p i c", p=128)),
                        waits=wv if b_ == 0 else (), inc="qg")
                qgw = P.cnt["qg"]
                for cc in range(2):
                    c = half * 2 + cc
                    ld("sync", slqc[:], slq_h[h:h + 1, c * 512:(c + 1) * 512],
                       waits=[("vector", P.cnt["vector"])])
                    qslq = P.cnt["qs"]
                    for i in range(NI):
                        wv2 = ()
                        if i == 0:
                            wv2 = [("qg", qgw), ("qs", qs_big3),
                                   ("scalar", P.cnt["scalar"]),
                                   ("vector", P.cnt["vector"])]
                        E("tensor", lambda eng, i_=i, cc_=cc: eng.matmul(
                            pC[:, :], wqh[:, i_ * 128:(i_ + 1) * 128],
                            big[:, i_ * 1024 + cc_ * 512:
                                i_ * 1024 + (cc_ + 1) * 512],
                            start=(i_ == 0), stop=(i_ == NI - 1)),
                            waits=wv2, inc="tensor")
                    tq_ = P.cnt["tensor"]
                    E("vector", lambda eng: eng.tensor_scalar(
                        qt_sb, pC[:, :], SCALE, None, ALU.mult),
                        waits=[("tensor", tq_)], inc="vector")
                    vqt = P.cnt["vector"]
                    E("scalar", lambda eng: eng.activation(
                        qsq, qt_sb, AF.Square), waits=[("vector", vqt)],
                        inc="scalar")
                    ssq = P.cnt["scalar"]
                    E("tensor", lambda eng: eng.matmul(
                        prow[:, :], onecb[:], qsq, start=True, stop=True),
                        waits=[("scalar", ssq), ("vector", vmx)], inc="tensor")
                    tbq = P.cnt["tensor"]
                    E("scalar", lambda eng: eng.activation(
                        brow, prow[:, :], AF.Sqrt, scale=mxk[0:1, 4:5]),
                        waits=[("tensor", tbq)], inc="scalar")
                    sbr = P.cnt["scalar"]
                    E("vector", lambda eng: eng.tensor_tensor(
                        crow, slqc[:], brow, ALU.subtract),
                        waits=[("scalar", sbr), ("qs", qslq)], inc="vector")
                    P.selfwait("vector")
                    E("vector", lambda eng: eng.tensor_copy(crow_b, crow),
                      inc="vector")
                    vcr = P.cnt["vector"]
                    nkt = 4 * c + 4
                    for kt in range(nkt):
                        off = max(0, (kt - 4 * c)) * 128
                        pS_ = pA if kt % 2 == 0 else pB
                        wv3 = [("vector", vcr)]
                        if kt >= 2:
                            wv3.append(("scalar", P.cnt["scalar"]))
                        E("tensor", lambda eng, kt_=kt, o_=off, p_=pS_:
                            eng.matmul(
                                p_[:, o_:512],
                                kt_r[:, kt_ * 128:(kt_ + 1) * 128],
                                qt_sb[:, o_:512], start=True, stop=False),
                            waits=wv3, inc="tensor")
                        E("tensor", lambda eng, o_=off, p_=pS_: eng.matmul(
                            p_[:, o_:512], on1b[:], crow_b[0:1, o_:512],
                            start=False, stop=True), inc="tensor")
                        tst = P.cnt["tensor"]
                        if kt >= 4 * c:
                            dcol = (kt - 4 * c) * 128
                            E("vector", lambda eng, d_=dcol, p_=pS_:
                                eng.tensor_tensor(
                                    p_[:, d_:d_ + 128], p_[:, d_:d_ + 128],
                                    tri[:], ALU.add),
                                waits=[("tensor", tst)], inc="vector")
                            wv4 = [("tensor", tst),
                                   ("vector", P.cnt["vector"])]
                        else:
                            wv4 = [("tensor", tst)]
                        E("scalar", lambda eng, kt_=kt, o_=off, p_=pS_, h_=h:
                            eng.activation(
                                pt_sb[:, kt_ * 512 + o_:(kt_ + 1) * 512],
                                p_[:, o_:512], AF.Exp,
                                bias=ramp[:, h_ * NT + kt_:h_ * NT + kt_ + 1]),
                            waits=wv4, inc="scalar")
                        sex = P.cnt["scalar"]
                        E("tensor", lambda eng, kt_=kt, o_=off, la=(kt == nkt - 1),
                            fi=(kt == 0): eng.matmul(
                                pD[:, o_:512], on128[:],
                                pt_sb[:, kt_ * 512 + o_:(kt_ + 1) * 512],
                                start=fi, stop=la),
                            waits=[("scalar", sex)], inc="tensor")
                        E("tensor", lambda eng, kt_=kt, o_=off, la=(kt == nkt - 1),
                            fi=(kt == 0): eng.matmul(
                                pC[:, o_:512],
                                v_r[:, kt_ * 128:(kt_ + 1) * 128],
                                pt_sb[:, kt_ * 512 + o_:(kt_ + 1) * 512],
                                start=fi, stop=la), inc="tensor")
                    tat = P.cnt["tensor"]
                    E("scalar", lambda eng: eng.activation(
                        rrow, pD[0:1, :], AF.Ln), waits=[("tensor", tat)],
                        inc="scalar")
                    sln = P.cnt["scalar"]
                    E("scalar", lambda eng: eng.activation(
                        rrow, rrow, AF.Exp, scale=-1.0),
                        waits=[("scalar", sln)], inc="scalar")
                    srr = P.cnt["scalar"]
                    E("vector", lambda eng: eng.tensor_copy(rrow_b, rrow),
                      waits=[("scalar", srr)], inc="vector")
                    vrr = P.cnt["vector"]
                    E("tensor", lambda eng: eng.matmul(
                        pD[:, :], on1b[:], rrow_b, start=True, stop=True),
                        waits=[("vector", vrr)], inc="tensor")
                    tbr = P.cnt["tensor"]
                    E("vector", lambda eng: eng.tensor_copy(brd, pD[:, :]),
                        waits=[("tensor", tbr)], inc="vector")
                    P.selfwait("vector")
                    E("vector", lambda eng: eng.tensor_tensor(
                        aotb, pC[:, :], brd, ALU.mult), inc="vector")
                    vao = P.cnt["vector"]
                    E("gpsimd", lambda eng, h_=h, c_=c: eng.dma_start(
                        out=aot_s[h_, :, c_ * 512:(c_ + 1) * 512], in_=aotb),
                        waits=[("vector", vao)], inc="qg")
                    qga = P.cnt["qg"]
                    E("vector", None, waits=[("qg", qga)])

        P.barrier()

        for half in range(2 if PHLIM >= 6 else 0):
            for i in range(NI):
                ld("sync", big[:, i * 1024:(i + 1) * 1024],
                   aot_s[i, :, half * 1024:(half + 1) * 1024],
                   waits=[("tensor", P.cnt["tensor"])] if i == 0 else ())
            qsa = P.cnt["qs"]
            for oc in range(8):
                wv = [("tensor", P.cnt["tensor"])]
                for b_ in range(4):
                    E("gpsimd", lambda eng, oc_=oc, b__=b_: eng.dma_start(
                        out=stg[:, b__ * 8 * 512:(b__ + 1) * 8 * 512].rearrange(
                            "p (i c) -> p i c", i=8),
                        in_=wo_h[b__ * 8 * 128:(b__ + 1) * 8 * 128,
                                 oc_ * 512:(oc_ + 1) * 512].rearrange(
                            "(i p) c -> p i c", p=128)),
                        waits=wv if b_ == 0 else (), inc="qg")
                qgo = P.cnt["qg"]
                for grp in range(2):
                    for i in range(NI):
                        for tt in range(4):
                            tq_off = (grp * 4 + tt) * 128
                            wv4 = []
                            if i == 0 and tt == 0:
                                wv4 = [("qg", qgo), ("qs", qsa),
                                       ("vector", P.cnt["vector"])]
                            E("tensor", lambda eng, i_=i, to_=tq_off, tt_=tt:
                                eng.matmul(
                                    pbank[tt_][:, :],
                                    big[:, i_ * 1024 + to_:i_ * 1024 + to_ + 128],
                                    stg[:, i_ * 512:(i_ + 1) * 512],
                                    start=(i_ == 0), stop=(i_ == NI - 1)),
                                waits=wv4, inc="tensor")
                    tmo = P.cnt["tensor"]
                    for tt in range(4):
                        t_ = half * 8 + grp * 4 + tt
                        par = tt % 2
                        first = (half == 0 and oc == 0 and grp == 0 and tt < 2)
                        wv5 = [] if first else [("vector", P.cnt["vector"])]
                        ld("sync", xsl[par][:],
                           x_h[t_ * 128:(t_ + 1) * 128,
                               oc * 512:(oc + 1) * 512], waits=wv5)
                        ld("sync", gsl[par][:],
                           g_s[t_ * 128:(t_ + 1) * 128,
                               oc * 512:(oc + 1) * 512])
                        qsx = P.cnt["qs"]
                        wv6 = [("tensor", tmo), ("qs", qsx)]
                        if not first:
                            wv6.append(("qg", P.cnt["qg"]))
                        E("vector", lambda eng, tt_=tt, p_=par:
                            eng.tensor_tensor(
                                gsl[p_][:], pbank[tt_][:, :], gsl[p_][:],
                                ALU.mult), waits=wv6, inc="vector")
                        E("vector", lambda eng, p_=par: eng.tensor_tensor(
                            obf[p_][:], gsl[p_][:], xsl[p_][:], ALU.add),
                            inc="vector")
                        vob = P.cnt["vector"]
                        E("gpsimd", lambda eng, t__=t_, oc_=oc, p_=par:
                            eng.dma_start(
                                out=out_h[t__ * 128:(t__ + 1) * 128,
                                          oc_ * 512:(oc_ + 1) * 512],
                                in_=obf[p_][:]),
                            waits=[("vector", vob)], inc="qg")

        E("gpsimd", None, waits=[("qg", P.cnt["qg"])])
        P.barrier()

        with nc.Block() as blk:
            @blk.sync
            def _(eng):
                P.replay("sync", eng, sems)

            @blk.gpsimd
            def _(eng):
                P.replay("gpsimd", eng, sems)

            @blk.tensor
            def _(eng):
                P.replay("tensor", eng, sems)

            @blk.vector
            def _(eng):
                P.replay("vector", eng, sems)

            @blk.scalar
            def _(eng):
                P.replay("scalar", eng, sems)

    return nc


def _consts():
    slopes = _alibi_slopes(NH)
    ident = np.eye(128, dtype=ml_dtypes.bfloat16)
    on1f = np.ones((1, 128), dtype=np.float32)
    onecb = np.ones((128, 1), dtype=ml_dtypes.bfloat16)
    on128 = np.ones((128, 128), dtype=ml_dtypes.bfloat16)
    kr = np.arange(128)[:, None]
    qr = np.arange(128)[None, :]
    tri = np.where(kr > qr, np.float32(NEG), np.float32(0.0)).astype(np.float32)
    ramp = np.zeros((128, NH * NT), dtype=np.float32)
    for h in range(NH):
        for kt in range(NT):
            ramp[:, h * NT + kt] = slopes[h] * (kt * 128 + np.arange(128))
    slq = (-slopes[:, None] * np.arange(S)[None, :]).astype(np.float32)
    on1b = np.ones((1, 128), dtype=ml_dtypes.bfloat16)
    fconst = np.ascontiguousarray(np.broadcast_to(np.array(
        [-0.5, 0.5, EPS, -1.0, 1.0, 0.0, SCALE, 127.0],
        dtype=np.float32)[None, :], (128, 8)))
    return dict(ident=ident, ones1f=on1f, ones1b=on1b, onecb=onecb,
                ones128=on128, trimask=tri, rampT=ramp, slopeq=slq,
                fconst=fconst)


_NC_CACHE = {}


def kernel(L0_attn_hidden_states, L0_attn_Wq, L0_attn_Wkv, L0_attn_Wo,
           L0_ffn_gamma, L0_ffn_beta, L0_ffn_w1, L0_ffn_b1, L0_ffn_w3,
           L0_ffn_b3):
    x = np.ascontiguousarray(np.asarray(L0_attn_hidden_states, dtype=np.float32))
    shared = dict(
        Wq=np.ascontiguousarray(np.asarray(L0_attn_Wq, np.float32)),
        Wkv=np.ascontiguousarray(np.asarray(L0_attn_Wkv, np.float32)),
        Wo=np.ascontiguousarray(np.asarray(L0_attn_Wo, np.float32)),
        w1=np.ascontiguousarray(np.asarray(L0_ffn_w1, np.float32)),
        w3=np.ascontiguousarray(np.asarray(L0_ffn_w3, np.float32)),
        gamma_r=np.ascontiguousarray(np.broadcast_to(
            np.asarray(L0_ffn_gamma, np.float32)[None, :], (128, D))),
        beta_r=np.ascontiguousarray(np.broadcast_to(
            np.asarray(L0_ffn_beta, np.float32)[None, :], (128, D))),
        b1_r=np.ascontiguousarray(np.broadcast_to(
            np.asarray(L0_ffn_b1, np.float32)[None, :], (128, D))),
        b3_r=np.ascontiguousarray(np.broadcast_to(
            np.asarray(L0_ffn_b3, np.float32)[None, :], (128, D))),
        **_consts(),
    )
    if "nc" not in _NC_CACHE:
        _NC_CACHE["nc"] = build_nc()
    nc = _NC_CACHE["nc"]
    in_maps = [dict(x=np.ascontiguousarray(x[b]), **shared) for b in range(8)]
    res = run_bass_kernel_spmd(nc, in_maps, core_ids=list(range(8)))
    return np.stack([r["out"] for r in res.results], axis=0)


if __name__ == "__main__":
    import reference as R
    ins = R.setup_inputs()
    out = kernel(**{k: np.asarray(v) for k, v in ins.items()})
    ref = np.asarray(R.reference(**ins))
    num = float(((np.asarray(out) - ref) ** 2).mean())
    den = float((ref ** 2).mean())
    print("var ratio:", num / den)
